# version 06
# baseline (speedup 1.0000x reference)
import math
from contextlib import ExitStack

import numpy as np
import ml_dtypes

import concourse.bass as bass
import concourse.mybir as mybir
from concourse.alu_op_type import AluOpType as OP
from concourse.bass_utils import run_bass_kernel_spmd

F32 = mybir.dt.float32
BF16 = mybir.dt.bfloat16
AF = mybir.ActivationFunctionType
AX = mybir.AxisListType
BF = ml_dtypes.bfloat16

NCORE = 8
T = 8192
D = 2048
NH = 16
DH = 128
CH = 512
NT = 1024
NCHUNK = 16
HFF = 5632
NE = 16
CSH = 12.0
NEGBIG = -1.0e9
ATTN_SCALE = 1.0 / math.sqrt(DH)
LN_EPS = 1e-5
Q_EPS = 1e-8

ENGS = ("tensor", "vector", "scalar", "gpsimd", "sync")


def own_chunks(c):
    return (c, NCHUNK - 1 - c)


def own_token_idx(c):
    a, b = own_chunks(c)
    return np.concatenate([np.arange(a * CH, (a + 1) * CH), np.arange(b * CH, (b + 1) * CH)])



def rope_tables(kind, positions):
    p = positions.astype(np.float64)
    freqs = 1.0 / 10000.0 ** (np.arange(0, DH, 2, dtype=np.float64) / DH)
    if kind == "il_a":
        freqs = freqs[(2 * np.arange(64)) % 64]
    ang = p[None, :] * freqs[:, None]
    cos = np.concatenate([np.cos(ang), np.cos(ang)], 0)
    sinm = np.concatenate([-np.sin(ang), np.sin(ang)], 0)
    return cos.astype(np.float32), sinm.astype(np.float32)


def deinterleave_perm():
    perm = np.empty(DH, dtype=np.int64)
    perm[:64] = 2 * np.arange(64)
    perm[64:] = 2 * np.arange(64) + 1
    return perm


def permute_head_cols(w, perm):
    nout = w.shape[-1]
    nh = nout // DH
    w2 = w.reshape(*w.shape[:-1], nh, DH)[..., perm]
    return np.ascontiguousarray(w2.reshape(*w.shape[:-1], nout))


def pack_bias(b):
    return np.ascontiguousarray(b.reshape(-1, 128).T.astype(np.float32))


def swap_halves(b128):
    return np.ascontiguousarray(np.concatenate([b128[64:128], b128[0:64]], axis=0))



class Prog:
    def __init__(self, name):
        self.name = name
        self.nc = bass.Bass("TRN2", target_bir_lowering=False)
        self.stream = {e: [] for e in ENGS}
        self.cnt = {}
        self.sem_names = []
        self.sbuf_req = []
        self.psum_req = []
        self.ap = {}
        self._sbuf_bytes = 0
        self.war = {}

    def dram(self, name, shape, dtype, kind):
        a = self.nc.dram_tensor(name, list(shape), dtype, kind=kind).ap()
        self.ap[name] = a
        return a

    def sbuf(self, name, shape, dtype):
        self.sbuf_req.append((name, list(shape), dtype))
        self._sbuf_bytes += shape[1] * mybir.dt.size(dtype)
        return name

    def psum(self, name, shape, dtype=F32):
        self.psum_req.append((name, list(shape), dtype))
        return name

    def sem(self, name):
        if name not in self.cnt:
            self.cnt[name] = 0
            self.sem_names.append(name)
        return name

    def emit(self, eng, fn, waits=(), inc=None, add=1):
        tok = None
        if inc is not None:
            self.sem(inc)
            self.cnt[inc] += add
            tok = (inc, self.cnt[inc])
        flat = []
        for w in waits:
            if w is None:
                continue
            if isinstance(w, list):
                flat.extend(x for x in w if x is not None)
            else:
                flat.append(w)
        ws = tuple(flat)
        self.stream[eng].append((ws, fn, (inc, add) if inc else None))
        return tok

    def dma(self, eng, out, in_, waits=(), inc=None, **kw):
        if inc is None:
            inc = "dms" if eng == "sync" else "dmg"
        return self.emit(eng, lambda e, ap: e.dma_start(out=_r(ap, out), in_=_r(ap, in_), **kw),
                         waits=waits, inc=inc, add=16)

    def now(self, sem):
        return (sem, self.cnt.get(sem, 0))

    def build(self):
        assert self._sbuf_bytes <= 209 * 1024, \
            f"{self.name}: sbuf {self._sbuf_bytes // 1024}KB/partition over budget"
        import bisect
        waited = {}
        for eng in ENGS:
            for waits, _, _ in self.stream[eng]:
                for (s, v) in waits:
                    waited.setdefault(s, set()).add(v)
        virt = {}
        kept_vals = {}
        keep_flags = {eng: [] for eng in ENGS}
        for eng in ENGS:
            for _, _, inc in self.stream[eng]:
                if inc is None:
                    keep_flags[eng].append(False)
                    continue
                s, add = inc
                v = virt.get(s, 0) + add
                virt[s] = v
                keep = (add == 16) or (v in waited.get(s, ()))
                keep_flags[eng].append(keep)
                if keep:
                    kept_vals.setdefault(s, []).append(v)
        addu = {}
        for eng in ENGS:
            for _, _, inc in self.stream[eng]:
                if inc is not None:
                    addu[inc[0]] = inc[1]

        nc = self.nc
        with ExitStack() as st:
            for nm, shape, dt in self.sbuf_req:
                self.ap[nm] = st.enter_context(nc.sbuf_tensor(nm, shape, dt))
            for nm, shape, dt in self.psum_req:
                self.ap[nm] = st.enter_context(nc.psum_tensor(nm, shape, dt))
            sems = {nm: st.enter_context(nc.semaphore(nm)) for nm in self.sem_names}
            if "PSA" in self.ap:
                self.ap["P0"] = self.ap["PSA"][:, 0:CH]
                self.ap["P1"] = self.ap["PSB"][:, 0:CH]
                self.ap["R0"] = self.ap["PRL"][0:1, 0:CH]
                self.ap["R1"] = self.ap["PRL"][64:65, 0:CH]
            block = st.enter_context(nc.Block())
            ap = self.ap

            def run(e, eng_name):
                last = {}
                for (waits, fn, inc), keep in zip(self.stream[eng_name], keep_flags[eng_name]):
                    for (s, v) in waits:
                        kv = kept_vals.get(s, [])
                        nv = bisect.bisect_right(kv, v) * addu.get(s, 1)
                        if nv > 0 and last.get(s, 0) < nv:
                            e.wait_ge(sems[s], nv)
                            last[s] = nv
                    ins = fn(e, ap)
                    if keep:
                        s, add = inc
                        ins.then_inc(sems[s], add)

            for eng_name in ENGS:
                if self.stream[eng_name]:
                    getattr(block, eng_name)(lambda e, _n=eng_name: run(e, _n))
        return nc


def _r(ap, x):
    if isinstance(x, tuple):
        nm, sl = x
        a = ap[nm]
        return a[sl] if sl is not None else a
    if isinstance(x, str):
        return ap[x][:, :]
    return x


def mm(out, lhsT, rhs, start, stop):
    return lambda e, ap: e.matmul(_r(ap, out), _r(ap, lhsT), _r(ap, rhs), start=start, stop=stop)


def tr(out, in_, ident):
    return lambda e, ap: e.transpose(_r(ap, out), _r(ap, in_), _r(ap, ident))


def act(out, in_, func, bias=None, scale=None, accum=None):
    def f(e, ap):
        kw = {}
        if bias is not None:
            kw["bias"] = _r(ap, bias) if isinstance(bias, (tuple, str)) else bias
        if scale is not None:
            kw["scale"] = _r(ap, scale) if isinstance(scale, (tuple, str)) else scale
        if accum is not None:
            kw["accum_out"] = _r(ap, accum)
        return e.activation(_r(ap, out), _r(ap, in_), func, **kw)
    return f


def tt(out, a, b, op):
    return lambda e, ap: e.tensor_tensor(_r(ap, out), _r(ap, a), _r(ap, b), op)


def ts1(out, a, s, op):
    def f(e, ap):
        sv = _r(ap, s) if isinstance(s, (tuple, str)) else s
        return e.tensor_single_scalar(_r(ap, out), _r(ap, a), sv, op)
    return f


def ts2(out, a, s1, s2, op0, op1):
    def f(e, ap):
        s1v = _r(ap, s1) if isinstance(s1, (tuple, str)) else s1
        s2v = _r(ap, s2) if isinstance(s2, (tuple, str)) else s2
        return e.tensor_scalar(_r(ap, out), _r(ap, a), s1v, s2v, op0, op1)
    return f


def stt(out, in0, scalar, in1, op0, op1):
    def f(e, ap):
        sv = _r(ap, scalar) if isinstance(scalar, (tuple, str)) else scalar
        return e.scalar_tensor_tensor(_r(ap, out), _r(ap, in0), sv, _r(ap, in1), op0, op1)
    return f


def red(out, in_, op, negate=False):
    return lambda e, ap: e.tensor_reduce(_r(ap, out), _r(ap, in_), AX.X, op, negate=negate)


def redmax(out, in_, negate=False):
    return red(out, in_, OP.max, negate)


def cp(out, in_):
    return lambda e, ap: e.tensor_copy(_r(ap, out), _r(ap, in_))


def mset(dst, val):
    return lambda e, ap: e.memset(_r(ap, dst), val)


def recip(out, in_):
    return lambda e, ap: e.reciprocal(_r(ap, out), _r(ap, in_))


def xsl(i, tg=None, lo=0, hi=None):
    if tg is None:
        return np.s_[:, i * NT + lo: i * NT + (hi if hi is not None else NT)]
    base = i * NT + tg * CH
    return np.s_[:, base + lo: base + (hi if hi is not None else CH)]



def declare_common(p, psum_big=6, need_xt=True, xt_out=False):
    p.psum("PSA", [128, 2 * CH], F32)
    p.psum("PSB", [128, 2 * CH], F32)
    for i in (3, 4, 5):
        p.psum("P%d" % i, [128, CH], F32)
    p.psum("PRL", [128, CH], F32)
    p.sbuf("ones1r", [1, 128], F32)
    p.sbuf("ones1rb", [1, 128], BF16)
    p.sbuf("onecol", [128, 1], F32)
    p.sbuf("onecol_b", [128, 1], BF16)
    p.sbuf("epscol", [1, 1], F32)
    p.sbuf("tinycol", [1, 1], F32)
    for nm, v in (("ones1r", 1.0), ("ones1rb", 1.0), ("onecol", 1.0), ("onecol_b", 1.0),
                  ("epscol", LN_EPS), ("tinycol", 1e-30)):
        p.emit("gpsimd", mset(nm, v), inc="ge")
    p.const_tok = p.now("ge")
    if need_xt:
        p.dram("xT_in", [D, NT], F32, "ExternalInput")
        p.sbuf("xT", [128, 16 * NT], F32)
        p.xt_tok = p.emit("sync", lambda e, ap: e.dma_start(
            out=ap["xT"][:, :].rearrange("p (i t) -> p i t", i=16),
            in_=ap["xT_in"].rearrange("(i p) t -> p i t", p=128)),
            inc="dms", add=16)
    if xt_out:
        p.dram("xT_out", [D, NT], F32, "ExternalOutput")


def finish_xt_out(p):
    tok = p.dma("sync", ("xT_out", None),
                (None, None), waits=[p.now("ve"), p.now("pe")], inc="dmo")
    p.stream["sync"][-1] = (p.stream["sync"][-1][0],
                            lambda e, ap: e.dma_start(
                                out=ap["xT_out"].rearrange("(i p) t -> p (i t)", p=128),
                                in_=ap["xT"][:, :]),
                            p.stream["sync"][-1][2])
    return tok


def final_waits(p, toks):
    def f(e, ap):
        return e.nop()
    p.emit("sync", f, waits=toks)


def emit_ln(p, xn_buf, first=False):
    if first:
        p.sbuf("sq0", [128, CH], F32)
        if getattr(p, "has_sq1", False):
            p.sbuf("sq1", [128, CH], F32)
        p.sbuf("mrow", [1, CH], F32)

        p.sbuf("vrow", [1, NT if getattr(p, "vrow_full", False) else CH], F32)
        p.sbuf("rrow", [1, CH], F32)
        p.sbuf("Mb", [128, CH], F32)
        p.sbuf("Rb", [128, CH], F32)
        p.sbuf("lntmp", [128, CH], F32)
    for tg in range(2):
        tsl = np.s_[0:1, 0:CH]
        vsl = np.s_[0:1, tg * CH:(tg + 1) * CH] if getattr(p, "vrow_full", False) \
            else np.s_[0:1, 0:CH]
        for i in range(16):
            sq = "sq%d" % (i % 2 if getattr(p, "has_sq1", False) else 0)
            tok_sq = p.emit("scalar", act(sq, ("xT", xsl(i, tg)), AF.Square),
                            waits=[p.xt_tok, p.war.get(sq)], inc="se")
            p.emit("tensor", mm(("R0", np.s_[0:1, 0:CH]), "onecol",
                                ("xT", xsl(i, tg)), start=(i == 0), stop=(i == 15)),
                   waits=[p.xt_tok, p.const_tok, p.war.get("R0") if i == 0 else None], inc="pe")
            tok_mm = p.emit("tensor", mm(("R1", np.s_[0:1, 0:CH]), "onecol", sq,
                                         start=(i == 0), stop=(i == 15)),
                            waits=[tok_sq, p.war.get("R1") if i == 0 else None], inc="pe")
            p.war[sq] = tok_mm
        tok_stats = p.now("pe")
        t1 = p.emit("vector", ts1(("mrow", tsl), ("R0", np.s_[0:1, 0:CH]), 1.0 / D, OP.mult),
                    waits=[tok_stats], inc="ve")
        t2 = p.emit("scalar", act(("rrow", tsl), ("mrow", tsl), AF.Square), waits=[t1], inc="se")
        t3 = p.emit("vector", stt(("vrow", vsl), ("R1", np.s_[0:1, 0:CH]), 1.0 / D, ("rrow", tsl),
                                  OP.mult, OP.subtract), waits=[t2, tok_stats], inc="ve")
        p.war["R0"] = t1
        p.war["R1"] = t3
        t4 = p.emit("scalar", act(("rrow", tsl), ("vrow", vsl), AF.Ln, bias=("epscol", np.s_[0:1, 0:1])),
                    waits=[t3], inc="se")
        t5 = p.emit("scalar", act(("rrow", tsl), ("rrow", tsl), AF.Exp, scale=-0.5), waits=[t4], inc="se")
        tm = p.emit("tensor", mm("P0", ("ones1r", np.s_[0:1, :]), ("mrow", tsl), start=True, stop=True),
                    waits=[t1, p.war.get("P0"), p.const_tok], inc="pe")
        tmv = p.emit("vector", cp("Mb", "P0"), waits=[tm], inc="ve")
        p.war["P0"] = tmv
        trm = p.emit("tensor", mm("P0", ("ones1r", np.s_[0:1, :]), ("rrow", tsl), start=True, stop=True),
                     waits=[t5, tmv], inc="pe")
        trv = p.emit("vector", cp("Rb", "P0"), waits=[trm], inc="ve")
        p.war["P0"] = trv
        for i in range(16):
            w1 = p.emit("vector", tt("lntmp", ("xT", xsl(i, tg)), "Mb", OP.subtract),
                        waits=[tmv], inc="ve")
            tok = p.emit("vector", tt((xn_buf, xsl(i, tg)), "lntmp", "Rb", OP.mult),
                         waits=[trv, w1, p.war.get(xn_buf) if (i == 0 and tg == 0) else None], inc="ve")
    p.war["Mb"] = tok
    p.war["Rb"] = tok
    return p.now("ve")


def stream_w(p, wb, src, waits=(), eng="gpsimd"):
    def f(e, ap):
        a = _r(p.ap, src)
        ncols = a.shape[-1]
        return e.dma_start(out=_r(p.ap, wb).rearrange("p (i j) -> p i j", i=16),
                           in_=a.rearrange("(i p) j -> p i j", p=128))
    return p.emit(eng, lambda e, ap: f(e, ap), waits=waits, inc="dmg" if eng == "gpsimd" else "dms", add=16)


def emit_proj_fm(p, xn, xn_tok, w_dram, col0, ncols, writer, wstr, wslot_cols, tag):
    nblocks = ncols // wslot_cols
    per = wslot_cols // 128
    slot_sz = 16 * wslot_cols
    ld = [None, None]
    for cb in range(nblocks):
        par = cb % 2
        src = (w_dram, np.s_[:, col0 + cb * wslot_cols: col0 + (cb + 1) * wslot_cols])
        ld[par] = stream_w(p, (wstr, np.s_[:, par * slot_sz:(par + 1) * slot_sz]), src,
                           waits=[p.war.get((wstr, par))])
        for oc4 in range(per):
            oc = cb * per + oc4
            for tg in range(2):
                ps = "P%d" % (oc * 2 + tg) if False else "P%d" % ((oc4 * 2 + tg) % 2)
                for i in range(16):
                    tok_mm = p.emit("tensor", mm(
                        ps, (wstr, np.s_[:, par * slot_sz + i * wslot_cols + oc4 * 128:
                                         par * slot_sz + i * wslot_cols + oc4 * 128 + 128]),
                        (xn, xsl(i, tg)), start=(i == 0), stop=(i == 15)),
                        waits=[ld[par], xn_tok, p.war.get(ps) if i == 0 else None], inc="pe")
                p.war[(wstr, par)] = tok_mm
                p.war[ps] = writer(oc, tg, ps, tok_mm)


def make_rope_writer(p, dst_fn, bias_sb, cos_sb, sin_sb, tag, first=False):
    if first:
        p.sbuf("rt", [128, CH], F32)
        p.sbuf("ra", [128, CH], F32)
        p.sbuf("rb", [128, CH], F32)

    def writer(oc, tg, ps, tok_mm):
        bias = (bias_sb, np.s_[:, oc:oc + 1]) if bias_sb else None
        t0 = p.emit("scalar", act("rt", ps, AF.Identity, bias=bias),
                    waits=[tok_mm, p.war.get("rt")], inc="se")
        csl = np.s_[:, tg * CH:(tg + 1) * CH]
        c0 = np.s_[0:64, tg * CH:(tg + 1) * CH]
        c1 = np.s_[64:128, tg * CH:(tg + 1) * CH]
        ta = p.emit("vector", tt("ra", "rt", (cos_sb, csl), OP.mult),
                    waits=[t0, p.war.get("ra"), p.rope_tok], inc="ve")
        tb0 = p.emit("vector", tt(("rb", np.s_[0:64, :]), ("rt", np.s_[64:128, :]),
                                  (sin_sb, c0), OP.mult), waits=[t0, p.war.get("rb")], inc="ve")
        tb1 = p.emit("vector", tt(("rb", np.s_[64:128, :]), ("rt", np.s_[0:64, :]),
                                  (sin_sb, c1), OP.mult), waits=[t0], inc="ve")
        p.war["rt"] = tb1
        dst = dst_fn(oc, tg)
        tok = p.emit("vector", tt(dst, "ra", "rb", OP.add), waits=[ta, tb0, tb1], inc="ve")
        p.war["ra"] = tok
        p.war["rb"] = tok
        return tok
    return writer


def make_plain_writer(p, dst_fn, bias_sb, out_dtype_note="bf16"):
    def writer(oc, tg, ps, tok_mm):
        bias = (bias_sb, np.s_[:, oc:oc + 1]) if bias_sb else None
        dst = dst_fn(oc, tg)
        return p.emit("scalar", act(dst, ps, AF.Identity, bias=bias), waits=[tok_mm], inc="se")
    return writer


def emit_proj_tok(p, xn, xn_tok, w_dram, col0, ncols, bvb, writer, wstr, wslot_cols, tag):
    nblocks = ncols // wslot_cols
    slot_sz = 16 * wslot_cols
    ld = [None, None]
    for cb in range(nblocks):
        par = cb % 2
        src = (w_dram, np.s_[:, col0 + cb * wslot_cols: col0 + (cb + 1) * wslot_cols])
        ld[par] = stream_w(p, (wstr, np.s_[:, par * slot_sz:(par + 1) * slot_sz]), src,
                           waits=[p.war.get((wstr, par))])
        for tt_ in range(8):
            ps = "P%d" % (tt_ % 2)
            for i in range(16):
                tok_mm = p.emit("tensor", mm(
                    (ps, np.s_[:, 0:wslot_cols]),
                    (xn, np.s_[:, i * NT + tt_ * 128: i * NT + tt_ * 128 + 128]),
                    (wstr, np.s_[:, par * slot_sz + i * wslot_cols:
                                 par * slot_sz + (i + 1) * wslot_cols]),
                    start=(i == 0), stop=(i == 15)),
                    waits=[ld[par], xn_tok, p.war.get(ps) if i == 0 else None], inc="pe")
            p.war[(wstr, par)] = tok_mm
            p.war[ps] = writer(cb, tt_, ps, tok_mm, wslot_cols)



def attn_slotlists(causal):
    if causal:
        l0 = [("int", s) for s in range(7)] + [("diag", None)]
        l1 = [("int", s) for s in range(15)] + [("diag", None)]
    else:
        l0 = [("int", s) for s in range(16)]
        l1 = [("int", s) for s in range(16)]
    return {0: l0, 1: l1}


def declare_attn(p, tag, causal, mqa=False, first=False):
    kp = 128 if mqa else D
    vw = 128 if mqa else D
    p.dram(f"KT_{tag}", [kp, T], BF16, "ExternalInput")
    p.dram(f"V_{tag}", [T, vw], BF16, "ExternalInput")
    if causal:
        p.dram(f"KTd_{tag}", [kp, NT], BF16, "ExternalInput")
        p.dram(f"Vd_{tag}", [NT, vw], BF16, "ExternalInput")
        p.dram(f"bsch_{tag}", [128, 24], F32, "ExternalInput")
        p.sbuf(f"bs_{tag}", [128, 24], F32)
        p.dma("gpsimd", f"bs_{tag}", f"bsch_{tag}")
        p.bias_tok = p.now("dmg")
    if first:
        if not mqa:
            p.sbuf("kbufH", [128, 2 * 8 * CH], BF16)
            p.sbuf("vbuf", [128, 2 * CH], BF16)
            if causal:
                p.sbuf("kdbuf", [128, NT], BF16)
                p.sbuf("vdbuf", [128, 8 * 128], BF16)
        p.sbuf("PT0", [128, 2 * CH], BF16)
        p.sbuf("PT1", [128, 2 * CH], BF16)
        p.sbuf("PT2", [128, 2 * CH], BF16)
        p.sbuf("ltmp", [128, CH], BF16)
        p.sbuf("lrow", [1, CH], F32)
        p.sbuf("lrowb", [1, CH], BF16)
        p.sbuf("linv", [128, CH], BF16)
        p.sbuf("trimask", [128, 128], BF16)
        p.sbuf("nshift", [128, 1], F32)
        p.emit("gpsimd", mset("nshift", -CSH), inc="ge")
        p.const_tok2 = p.now("ge")
        p.dram("tri_in", [128, 128], F32, "ExternalInput")
        p.dma("gpsimd", "trimask", "tri_in")
        p.tri_tok = p.now("dmg")


def load_mqa_kv(p, tag):
    first = "kmq" not in [n for n, _, _ in p.sbuf_req]
    if first:
        p.sbuf("kmq", [128, T], BF16)
        p.sbuf("vmq", [128, 64 * 128], BF16)
        p.sbuf("kmqd", [128, NT], BF16)
        p.sbuf("vmqd", [128, 8 * 128], BF16)
    p.dma("sync", "kmq", (f"KT_{tag}", None), waits=[p.war.get("kmq")])
    p.emit("sync", lambda e, ap: e.dma_start(
        out=ap["vmq"][:, :].rearrange("p (c d) -> p c d", d=128),
        in_=ap[f"V_{tag}"].rearrange("(c p) d -> p c d", p=128)),
        waits=[w for w in [p.war.get("vmq")] if w], inc="dms", add=16)
    p.dma("sync", "kmqd", (f"KTd_{tag}", None), waits=[p.war.get("kmqd")])
    p.emit("sync", lambda e, ap: e.dma_start(
        out=ap["vmqd"][:, :].rearrange("p (c d) -> p c d", d=128),
        in_=ap[f"Vd_{tag}"].rearrange("(c p) d -> p c d", p=128)),
        waits=[w for w in [p.war.get("vmqd")] if w], inc="dms", add=16)
    return p.now("dms")


def emit_attn(p, tag, qt_buf, qt_tok, attnT, causal, nheads=NH, mqa=False,
              kv_res_tok=None):
    n_int = {0: 7, 1: 15} if causal else {0: 16, 1: 16}
    max_kb = max(n_int.values())
    resident = mqa or nheads == 1
    bias_tok = p.bias_tok if causal else p.const_tok2
    pp = 0
    pending = []

    def flush_one():
        d = pending.pop(0)
        pt = d["pt"]
        if len(d["items"]) == 2:
            tl = p.emit("vector", tt("ltmp", (pt, np.s_[:, 0:CH]), (pt, np.s_[:, CH:2 * CH]),
                                     OP.add), waits=[d["ready"], p.war.get("ltmp")], inc="ve")
            lsrc, lsl = "ltmp", np.s_[:, 0:CH]
            t_l = p.emit("tensor", mm((d["plr"], np.s_[0:1, 0:CH]), "onecol_b", (lsrc, lsl),
                                      start=d["first_l"], stop=d["last_l"]),
                         waits=[tl, p.const_tok,
                                p.war.get(d["plr"]) if d["first_l"] else None], inc="pe")
            p.war["ltmp"] = t_l
        else:
            (qlo, col, _, _) = d["items"][0][:4]
            t_l = p.emit("tensor", mm((d["plr"], np.s_[0:1, qlo:CH]), "onecol_b",
                                      (pt, np.s_[:, col * CH + qlo:(col + 1) * CH]),
                                      start=d["first_l"], stop=d["last_l"]),
                         waits=[d["ready"], p.const_tok,
                                p.war.get(d["plr"]) if d["first_l"] else None], inc="pe")
        for (qlo, col, lhs_v, fl) in d["items"]:
            t_pv = p.emit("tensor", mm((d["po"], np.s_[:, qlo:CH]), lhs_v,
                                       (pt, np.s_[:, col * CH + qlo:(col + 1) * CH]),
                                       start=fl[0], stop=fl[1]),
                          waits=[d["ready"], p.war.get(d["po"]) if fl[0] else None], inc="pe")
        p.war[pt] = t_pv
        for vs in d["vslots"]:
            p.war[("vbuf", vs)] = t_pv

    def emit_pair(items, lc, colx, ld_toks, diag):
        nonlocal pp
        while len(pending) >= 3:
            flush_one()
        psp = ("PSA", "PSB")[pp % 2]
        pt = "PT%d" % (pp % 3)
        toks_s = []
        for ci, (qlo, lhs_k, lhs_v, vslot) in enumerate(items):
            qsl = (qt_buf, np.s_[:, items_h * NT + lc * CH + qlo: items_h * NT + (lc + 1) * CH])
            toks_s.append(p.emit("tensor", mm((psp, np.s_[:, ci * CH + qlo:(ci + 1) * CH]),
                                              lhs_k, qsl, start=True, stop=True),
                                 waits=[ld_toks, qt_tok, p.war.get(psp) if ci == 0 else None],
                                 inc="pe"))
        if len(items) == 2:
            esl = np.s_[:, 0:2 * CH]
        else:
            qlo = items[0][0]
            esl = np.s_[:, qlo:CH]
        t_e = p.emit("scalar", act((pt, esl), (psp, esl), AF.Exp, scale=ATTN_SCALE,
                                   bias=(f"bs_{tag}", np.s_[:, colx:colx + 1]) if causal
                                   else ("nshift", np.s_[:, 0:1])),
                     waits=toks_s + [p.war.get(pt), bias_tok], inc="se")
        p.war[psp] = t_e
        ready = t_e
        if diag:
            qlo = items[0][0]
            ready = p.emit("gpsimd", tt((pt, np.s_[:, qlo:qlo + 128]),
                                        (pt, np.s_[:, qlo:qlo + 128]), "trimask", OP.mult),
                           waits=[t_e, p.tri_tok], inc="ge")
        ivs = []
        itspec = []
        for i, (qlo, lhs_k, lhs_v, vslot) in enumerate(items):
            first = not started[lc]
            started[lc] = True
            last = (kcleft[lc] == 1)
            kcleft[lc] -= 1
            itspec.append((qlo, i, lhs_v, (first, last)))
            if vslot is not None:
                ivs.append(vslot)
        pending.append(dict(pt=pt, items=itspec, plr="R%d" % lc, po="P%d" % (4 + lc),
                            ready=ready, vslots=ivs,
                            first_l=itspec[0][3][0], last_l=itspec[-1][3][1]))
        pp += 1

    kh_tok = [None, None]
    for h in range(nheads):
        hs = 0 if resident else h * 128
        items_h = h
        if not resident:
            for hf in range(2):
                kh_tok[hf] = p.dma("sync", ("kbufH", np.s_[:, hf * 8 * CH:(hf + 1) * 8 * CH]),
                                   (f"KT_{tag}", np.s_[hs:hs + 128, hf * 8 * CH:(hf + 1) * 8 * CH]),
                                   waits=[p.war.get(("kbufH", hf))])
            if causal:
                kd_tok = p.dma("sync", "kdbuf", (f"KTd_{tag}", np.s_[hs:hs + 128, :]),
                               waits=[p.war.get("kdbuf")])
                vd_tok = p.emit("scalar", lambda e, ap, _hs=hs: e.dma_start(
                    out=ap["vdbuf"][:, :].rearrange("p (c d) -> p c d", d=128),
                    in_=ap[f"Vd_{tag}"][:, _hs:_hs + 128].rearrange("(c p) d -> p c d", p=128)),
                    waits=[w for w in [p.war.get("vdbuf")] if w], inc="dmc", add=16)
        started = {0: False, 1: False}
        kcleft = {lc: (n_int[lc] + (1 if causal else 0)) * 4 for lc in (0, 1)}
        for kb in range(max_kb):
            lcs = [lc for lc in (0, 1) if kb < n_int[lc]]
            if not lcs:
                continue
            if resident:
                ld_toks, vslot = kv_res_tok, None
                kb_k = ("kmq", kb * CH)
                kb_v = ("vmq", kb * 4 * 128)
            else:
                vslot = kb % 2
                v_tok = p.emit("scalar", lambda e, ap, _kb=kb, _hs=hs, _sl=vslot: e.dma_start(
                    out=ap["vbuf"][:, _sl * CH:(_sl + 1) * CH].rearrange("p (c d) -> p c d", d=128),
                    in_=ap[f"V_{tag}"][_kb * CH:(_kb + 1) * CH, _hs:_hs + 128]
                    .rearrange("(c p) d -> p c d", p=128)),
                    waits=[w for w in [p.war.get(("vbuf", vslot))] if w], inc="dmc", add=16)
                hf = kb // 8
                ld_toks = [kh_tok[hf], v_tok]
                kb_k = ("kbufH", hf * 8 * CH + (kb % 8) * CH)
                kb_v = ("vbuf", vslot * CH)
            for lc in lcs:
                for jp in range(2):
                    items = []
                    for j in (2 * jp, 2 * jp + 1):
                        lhs_k = (kb_k[0], np.s_[:, kb_k[1] + j * 128: kb_k[1] + j * 128 + 128])
                        lhs_v = (kb_v[0], np.s_[:, kb_v[1] + j * 128: kb_v[1] + j * 128 + 128])
                        items.append((0, lhs_k, lhs_v, vslot))
                    colx = (kb if lc == 0 else 8 + kb) if causal else 0
                    emit_pair(items, lc, colx, ld_toks, False)
            if not resident:
                p.war[("kbufH", kb // 8)] = p.now("pe")
        if causal:
            for lc in (0, 1):
                for j in range(4):
                    if resident:
                        lhs_k = ("kmqd", np.s_[:, lc * CH + j * 128: lc * CH + j * 128 + 128])
                        lhs_v = ("vmqd", np.s_[:, (lc * 4 + j) * 128: (lc * 4 + j + 1) * 128])
                        dtoks = kv_res_tok
                    else:
                        lhs_k = ("kdbuf", np.s_[:, lc * CH + j * 128: lc * CH + j * 128 + 128])
                        lhs_v = ("vdbuf", np.s_[:, (lc * 4 + j) * 128: (lc * 4 + j + 1) * 128])
                        dtoks = [kd_tok, vd_tok]
                    colx = 7 if lc == 0 else 23
                    emit_pair([(j * 128, lhs_k, lhs_v, None)], lc, colx, dtoks, True)
        while pending:
            flush_one()
        if not resident and causal:
            p.war["kdbuf"] = p.now("pe")
            p.war["vdbuf"] = p.now("pe")
        for lc in (0, 1):
            plr, po = "R%d" % lc, "P%d" % (4 + lc)
            tok_pe = p.now("pe")
            t_r1 = p.emit("scalar", act(("lrow", np.s_[0:1, :]), (plr, np.s_[0:1, 0:CH]), AF.Ln),
                          waits=[tok_pe, p.war.get("lrow")], inc="se")
            p.war[plr] = t_r1
            t_r2 = p.emit("scalar", act(("lrowb", np.s_[0:1, :]), ("lrow", np.s_[0:1, :]),
                                        AF.Exp, scale=-1.0),
                          waits=[t_r1, p.war.get("lrowb")], inc="se")
            p.war["lrow"] = t_r2
            t_b = p.emit("tensor", mm("P3", ("ones1rb", np.s_[0:1, :]), ("lrowb", np.s_[0:1, :]),
                                      start=True, stop=True),
                         waits=[t_r2, p.war.get("P3")], inc="pe")
            p.war["lrowb"] = t_b
            t_c = p.emit("scalar", act("linv", "P3", AF.Identity),
                         waits=[t_b, p.war.get("linv")], inc="se")
            p.war["P3"] = t_c
            tok = p.emit("vector", tt((attnT, np.s_[:, h * NT + lc * CH:h * NT + (lc + 1) * CH]),
                                      po, "linv", OP.mult),
                         waits=[t_c, tok_pe], inc="ve")
            p.war["linv"] = tok
            p.war[po] = tok
    p.war[qt_buf] = p.now("pe")
    return p.now("ve")


def emit_out_proj(p, attnT, attn_tok, w_dram, bias_sb, wstr, tag, wcols=256):
    ld = [None, None]
    per = wcols // 128
    slot = 16 * wcols
    for cb in range(D // wcols):
        par = cb % 2
        wsrc = (w_dram, np.s_[:, cb * wcols:(cb + 1) * wcols])
        ld[par] = stream_w(p, (wstr, np.s_[:, par * slot:(par + 1) * slot]), wsrc,
                           waits=[p.war.get((wstr, par))])
        for oc4 in range(per):
            oc = cb * per + oc4
            for tg in range(2):
                ps = "P%d" % ((oc4 * 2 + tg) % 2)
                for i in range(16):
                    tok_mm = p.emit("tensor", mm(
                        ps, (wstr, np.s_[:, par * slot + i * wcols + oc4 * 128:
                                         par * slot + i * wcols + oc4 * 128 + 128]),
                        (attnT, xsl(i, tg)), start=(i == 0), stop=(i == 15)),
                        waits=[ld[par], attn_tok, p.war.get(ps) if i == 0 else None], inc="pe")
                p.war[(wstr, par)] = tok_mm
                if bias_sb:
                    tok = p.emit("vector", stt(("xT", xsl(oc, tg)), ps,
                                               (bias_sb, np.s_[:, oc:oc + 1]), ("xT", xsl(oc, tg)),
                                               OP.add, OP.add),
                                 waits=[tok_mm], inc="ve")
                else:
                    tok = p.emit("vector", tt(("xT", xsl(oc, tg)), ps, ("xT", xsl(oc, tg)), OP.add),
                                 waits=[tok_mm], inc="ve")
                p.war[ps] = tok
    p.war[attnT] = p.now("pe")
    return p.now("ve")



def declare_ffn(p, first=True):
    p.dram("Q1T", [D, HFF], BF16, "ExternalInput")
    p.dram("Q2T", [D, HFF], BF16, "ExternalInput")
    p.dram("Q3T", [HFF, D], BF16, "ExternalInput")
    if first:
        pass



W_SLICES = (("w1s", HFF, 256), ("w2s", HFF, 256), ("w3s", 256, HFF))


def slice_tiles(r, c_):
    for t_ in range(r // 128):
        for cb in range((c_ + 511) // 512):
            cw = min(512, c_ - cb * 512)
            yield t_, cb, cw


def build_L0():
    p = Prog("L0")
    for nm, r, c_ in W_SLICES:
        p.dram(nm, [r, c_], F32, "ExternalInput")
    p.dram("partials", [1, 4], F32, "ExternalOutput")
    p.psum("R0", [1, 4], F32)
    p.sbuf("wt0", [128, 512], F32)
    p.sbuf("wt1", [128, 512], F32)
    p.sbuf("acc", [128, 4], F32)
    p.sbuf("rtmp", [128, 1], F32)
    p.sbuf("onecol", [128, 1], F32)
    p.sbuf("prow", [1, 4], F32)
    p.emit("gpsimd", mset("onecol", 1.0), inc="ge")
    p.emit("gpsimd", mset("acc", 0.0), inc="ge")
    ctok = p.now("ge")
    ti = 0
    for wi, (nm, r, c_) in enumerate(W_SLICES):
        for t_, cb, cw in slice_tiles(r, c_):
            par = ti % 2
            ti += 1
            wt = "wt%d" % par
            ld = p.dma("gpsimd", (wt, np.s_[:, 0:cw]),
                       (nm, np.s_[t_ * 128:(t_ + 1) * 128, cb * 512:cb * 512 + cw]),
                       waits=[p.war.get((wt,))])
            t_r = p.emit("vector", lambda e, ap, _w=wt, _cw=cw: e.tensor_reduce(
                ap["rtmp"][:, 0:1], ap[_w][:, 0:_cw], AX.X, OP.add, apply_absolute_value=True),
                waits=[ld, ctok, p.war.get("rtmp")], inc="ve")
            p.war[(wt,)] = t_r
            tok = p.emit("vector", tt(("acc", np.s_[:, wi:wi + 1]), ("acc", np.s_[:, wi:wi + 1]),
                                      ("rtmp", np.s_[:, 0:1]), OP.add),
                         waits=[t_r], inc="ve")
            p.war["rtmp"] = tok
    tok_mm = p.emit("tensor", mm(("R0", np.s_[0:1, 0:4]), "onecol", ("acc", np.s_[:, 0:4]),
                                 start=True, stop=True),
                    waits=[p.now("ve"), ctok], inc="pe")
    tok_cp = p.emit("vector", cp(("prow", np.s_[0:1, 0:4]), ("R0", np.s_[0:1, 0:4])),
                    waits=[tok_mm], inc="ve")
    out_tok = p.dma("sync", ("partials", None), ("prow", np.s_[0:1, 0:4]), waits=[tok_cp], inc="dmo")
    p.emit("sync", lambda e, ap: e.nop(), waits=[out_tok])
    return p


def emit_quantize(p, gp_tok):
    t_g = p.emit("tensor", mm(("R0", np.s_[0:1, 0:4]), ("onecol", np.s_[0:8, 0:1]),
                              ("gpart_sb", np.s_[0:8, 0:4]), start=True, stop=True),
                 waits=[gp_tok, p.const_tok, p.war.get("R0")], inc="pe")
    NW = float(HFF * D)
    t_s = p.emit("vector", ts2(("srow", np.s_[0:1, 0:4]), ("R0", np.s_[0:1, 0:4]),
                               1.0 / NW, Q_EPS, OP.mult, OP.add), waits=[t_g], inc="ve")
    p.war["R0"] = t_s
    t_rc = p.emit("vector", recip(("srow", np.s_[0:1, 0:4]), ("srow", np.s_[0:1, 0:4])),
                  waits=[t_s], inc="ve")
    for j in range(3):
        t_b = p.emit("tensor", mm(("P3", np.s_[:, 0:1]), ("ones1r", np.s_[0:1, :]),
                                  ("srow", np.s_[0:1, j:j + 1]), start=True, stop=True),
                     waits=[t_rc, p.war.get("P3")], inc="pe")
        t_c = p.emit("scalar", act(("scol", np.s_[:, j:j + 1]), ("P3", np.s_[:, 0:1]), AF.Identity),
                     waits=[t_b], inc="se")
        t_n = p.emit("scalar", act(("nscol", np.s_[:, j:j + 1]), ("P3", np.s_[:, 0:1]),
                                   AF.Identity, scale=-1.0), waits=[t_b], inc="se")
        p.war["P3"] = t_n
    sc_tok = p.now("se")
    p.q_out_toks = []
    ti = 0
    for wi, (nm, r, c_) in enumerate(W_SLICES):
        ntile = r // 128
        out_nm = "q%dT_out" % (wi + 1)
        for t_, cb, cw in slice_tiles(r, c_):
            par = ti % 2
            ti += 1
            wt = "wq%d" % par
            ld = p.dma("gpsimd", (wt, np.s_[:, 0:cw]),
                       (nm, np.s_[t_ * 128:(t_ + 1) * 128, cb * 512:cb * 512 + cw]),
                       waits=[p.war.get((wt,))])
            ta = p.emit("vector", ts2(("qa", np.s_[:, 0:cw]), (wt, np.s_[:, 0:cw]),
                                      ("scol", np.s_[:, wi:wi + 1]), 0.5, OP.mult, OP.is_ge),
                        waits=[ld, sc_tok, p.war.get("qa")], inc="ve")
            tb = p.emit("vector", ts2(("qb", np.s_[:, 0:cw]), (wt, np.s_[:, 0:cw]),
                                      ("nscol", np.s_[:, wi:wi + 1]), 0.5, OP.mult, OP.is_ge),
                        waits=[ld, sc_tok, p.war.get("qb")], inc="ve")
            p.war[(wt,)] = tb
            tq = p.emit("vector", tt(("qq", np.s_[:, 0:cw]), ("qa", np.s_[:, 0:cw]),
                                     ("qb", np.s_[:, 0:cw]), OP.subtract),
                        waits=[ta, tb, p.war.get("qq")], inc="ve")
            p.war["qa"] = tq
            p.war["qb"] = tq
            for sub4 in range(cw // 128):
                subg = cb * 4 + sub4
                t_t = p.emit("tensor", tr(("P3", np.s_[:, 0:128]),
                                          ("qq", np.s_[:, sub4 * 128:(sub4 + 1) * 128]),
                                          "ident_f"),
                             waits=[tq, p.ident_tok, p.war.get("P3")], inc="pe")
                t_e = p.emit("vector", cp(("qst", np.s_[:, (subg * ntile + t_) * 128:
                                                        (subg * ntile + t_) * 128 + 128]),
                                          ("P3", np.s_[:, 0:128])),
                             waits=[t_t, p.war.get("qst")], inc="ve")
                p.war["P3"] = t_e
            p.war["qq"] = p.now("pe")
        st_tok = p.now("ve")
        out_tok = p.emit("sync", lambda e, ap, _o=out_nm, _n=(c_ // 128) * r, _o2=r: e.dma_start(
            out=ap[_o].rearrange("(s p) t -> p s t", p=128),
            in_=ap["qst"][:, 0:_n].rearrange("p (s t) -> p s t", t=_o2)),
            waits=[st_tok], inc="dmo", add=16)
        p.war["qst"] = out_tok
        p.q_out_toks.append(out_tok)



def declare_l1ffn_A(p, first=True):
    p.dram("sim", [D, NE], F32, "ExternalInput")
    p.dram("gates", [1, NE], F32, "ExternalInput")
    for nm in ("qproj", "kproj", "vproj"):
        p.dram(nm, [NE, D, DH], F32, "ExternalInput")
    p.dram("onehot", [NE, NE * 128], F32, "ExternalInput")
    p.dram("w_out", [NT, NE], F32, "ExternalOutput")
    p.dram("qTf_out", [128, NT], BF16, "ExternalOutput")
    p.dram("kTf_out", [128, NT], BF16, "ExternalOutput")
    p.dram("vf_out", [NT, 128], BF16, "ExternalOutput")
    if first:
        p.sbuf("simb", [128, 16 * NE], F32)
        p.sbuf("simn", [128, 16 * NE], BF16)
        p.sbuf("sqs", [128, NE], F32)
        p.sbuf("ssrow", [1, NE], F32)
        p.sbuf("rnb", [128, NE], F32)
        p.sbuf("sgr", [1, NE], F32)
        p.sbuf("sgb", [128, NE], F32)
        p.sbuf("rnt", [1, NT], F32)
        p.sbuf("ar", [1, NT], F32)
        p.sbuf("rnb2", [128, CH], F32)
        p.sbuf("rncols", [128, 8], F32)
        p.sbuf("lgt", [128, NE], F32)
        p.sbuf("g1", [128, NE], F32)
        p.sbuf("g2", [128, NE], F32)
        p.sbuf("m1c", [128, 1], F32)
        p.sbuf("m2c", [128, 1], F32)
        p.sbuf("ibit", [128, 1], F32)
        p.sbuf("wtok", [128, 8 * NE], F32)
        p.sbuf("wT", [16, NT], F32)
        p.sbuf("oh", [16, NE * 128], F32)
        p.sbuf("mxw0", [128, 16 * 128], BF16)
        p.sbuf("mxw1", [128, 16 * 128], BF16)
        p.sbuf("wbe", [128, CH], F32)
        p.sbuf("macc", [128, NT], F32)
        p.sbuf("mxt", [128, CH], F32)
        p.sbuf("qfb", [128, NT], BF16)
        p.sbuf("vstg", [128, 128], BF16)


def emit_l1ffn_A(p, xn, xn_tok):
    ld_sim = p.emit("gpsimd", lambda e, ap: e.dma_start(
        out=ap["simb"][:, :].rearrange("p (i e) -> p i e", i=16),
        in_=ap["sim"].rearrange("(i p) e -> p i e", p=128)),
        inc="dmg", add=16)
    ld_g = p.dma("gpsimd", ("sgr", np.s_[0:1, 0:NE]), ("gates", None))
    ld_oh = p.dma("gpsimd", ("oh", np.s_[0:16, :]), ("onehot", None))
    for i in range(16):
        t_sq = p.emit("scalar", act("sqs", ("simb", np.s_[:, i * NE:(i + 1) * NE]), AF.Square),
                      waits=[ld_sim, p.war.get("sqs")], inc="se")
        t_mm = p.emit("tensor", mm(("R0", np.s_[0:1, 0:NE]), "onecol", "sqs",
                                   start=(i == 0), stop=(i == 15)),
                      waits=[t_sq, p.const_tok, p.war.get("R0") if i == 0 else None], inc="pe")
        p.war["sqs"] = t_mm
    t1 = p.emit("scalar", act(("ssrow", np.s_[0:1, 0:NE]), ("R0", np.s_[0:1, 0:NE]), AF.Ln,
                              bias=("tinycol", np.s_[0:1, 0:1])), waits=[p.now("pe")], inc="se")
    p.war["R0"] = t1
    t2 = p.emit("scalar", act(("ssrow", np.s_[0:1, 0:NE]), ("ssrow", np.s_[0:1, 0:NE]),
                              AF.Exp, scale=-0.5), waits=[t1], inc="se")
    t3 = p.emit("tensor", mm(("P3", np.s_[:, 0:NE]), ("ones1r", np.s_[0:1, :]),
                             ("ssrow", np.s_[0:1, 0:NE]), start=True, stop=True),
                waits=[t2, p.war.get("P3")], inc="pe")
    t4 = p.emit("vector", cp("rnb", ("P3", np.s_[:, 0:NE])), waits=[t3], inc="ve")
    p.war["P3"] = t4
    for i in range(16):
        t5 = p.emit("vector", tt(("simn", np.s_[:, i * NE:(i + 1) * NE]),
                                 ("simb", np.s_[:, i * NE:(i + 1) * NE]), "rnb", OP.mult),
                    waits=[t4], inc="ve")
    simn_tok = t5
    t6 = p.emit("scalar", act(("sgr", np.s_[0:1, 0:NE]), ("sgr", np.s_[0:1, 0:NE]), AF.Sigmoid),
                waits=[ld_g], inc="se")
    t7 = p.emit("tensor", mm(("P3", np.s_[:, 0:NE]), ("ones1r", np.s_[0:1, :]),
                             ("sgr", np.s_[0:1, 0:NE]), start=True, stop=True),
                waits=[t6, p.war.get("P3")], inc="pe")
    t8 = p.emit("vector", cp("sgb", ("P3", np.s_[:, 0:NE])), waits=[t7], inc="ve")
    p.war["P3"] = t8
    r1 = p.emit("vector", ts1(("ar", np.s_[0:1, 0:NT]), ("vrow", np.s_[0:1, 0:NT]), LN_EPS, OP.add),
                waits=[p.ln_done], inc="ve")
    r2 = p.emit("vector", recip(("ar", np.s_[0:1, 0:NT]), ("ar", np.s_[0:1, 0:NT])), waits=[r1], inc="ve")
    r3 = p.emit("vector", tt(("rnt", np.s_[0:1, 0:NT]), ("vrow", np.s_[0:1, 0:NT]),
                             ("ar", np.s_[0:1, 0:NT]), OP.mult), waits=[r2], inc="ve")
    r4 = p.emit("vector", ts2(("rnt", np.s_[0:1, 0:NT]), ("rnt", np.s_[0:1, 0:NT]),
                              float(D), 1e-30, OP.mult, OP.add), waits=[r3], inc="ve")
    r5 = p.emit("scalar", act(("rnt", np.s_[0:1, 0:NT]), ("rnt", np.s_[0:1, 0:NT]), AF.Ln,
                              bias=("tinycol", np.s_[0:1, 0:1])), waits=[r4], inc="se")
    r6 = p.emit("scalar", act(("rnt", np.s_[0:1, 0:NT]), ("rnt", np.s_[0:1, 0:NT]),
                              AF.Exp, scale=-0.5), waits=[r5], inc="se")
    for tg in range(2):
        tb = p.emit("tensor", mm("P3", ("ones1r", np.s_[0:1, :]),
                                 ("rnt", np.s_[0:1, tg * CH:(tg + 1) * CH]), start=True, stop=True),
                    waits=[r6, p.war.get("P3")], inc="pe")
        tc = p.emit("vector", cp("rnb2", "P3"), waits=[tb, p.war.get("rnb2")], inc="ve")
        p.war["P3"] = tc
        for t4_ in range(4):
            tt_ = tg * 4 + t4_
            td = p.emit("tensor", tr(("P5", np.s_[:, 0:128]),
                                     ("rnb2", np.s_[:, t4_ * 128:(t4_ + 1) * 128]), "ident_f"),
                        waits=[tc, p.ident_tok, p.war.get("P5")], inc="pe")
            te = p.emit("scalar", act(("rncols", np.s_[:, tt_:tt_ + 1]), ("P5", np.s_[:, 0:1]),
                                      AF.Identity), waits=[td], inc="se")
            p.war["P5"] = te
        p.war["rnb2"] = p.now("pe")
    rn_tok = p.now("se")
    for tt_ in range(8):
        for i in range(16):
            t_mm = p.emit("tensor", mm(("P0", np.s_[:, 0:NE]),
                                       (xn, np.s_[:, i * NT + tt_ * 128: i * NT + tt_ * 128 + 128]),
                                       ("simn", np.s_[:, i * NE:(i + 1) * NE]),
                                       start=(i == 0), stop=(i == 15)),
                          waits=[xn_tok, simn_tok, p.war.get("P0") if i == 0 else None], inc="pe")
        g0 = p.emit("scalar", act("lgt", ("P0", np.s_[:, 0:NE]), AF.Identity,
                                  scale=("rncols", np.s_[:, tt_:tt_ + 1])),
                    waits=[t_mm, rn_tok, p.war.get("lgt")], inc="se")
        p.war["P0"] = g0
        g1 = p.emit("vector", tt("lgt", "lgt", "sgb", OP.subtract), waits=[g0, t8], inc="ve")
        g2 = p.emit("vector", redmax("m1c", "lgt"), waits=[g1, p.war.get("m1c")], inc="ve")
        g3 = p.emit("vector", ts1("ibit", ("m1c", np.s_[:, 0:1]), 0.0, OP.is_le),
                    waits=[g2, p.war.get("ibit")], inc="ve")
        g4 = p.emit("vector", ts1("g1", "lgt", ("m1c", np.s_[:, 0:1]), OP.is_equal),
                    waits=[g2, p.war.get("g1")], inc="ve")
        g5 = p.emit("vector", stt("g2", "g1", NEGBIG, "lgt", OP.mult, OP.add),
                    waits=[g4, p.war.get("g2")], inc="ve")
        g6 = p.emit("vector", redmax("m2c", "g2"), waits=[g5, p.war.get("m2c")], inc="ve")
        g7 = p.emit("vector", ts1("g1", "lgt", ("m2c", np.s_[:, 0:1]), OP.is_ge),
                    waits=[g6, g5], inc="ve")
        g8 = p.emit("vector", ts1("g2", "lgt", 0.0, OP.is_gt), waits=[g6], inc="ve")
        g9 = p.emit("vector", stt("g1", "g1", ("ibit", np.s_[:, 0:1]), "g2", OP.mult, OP.add),
                    waits=[g7, g8, g3], inc="ve")
        ga = p.emit("scalar", act("g2", "lgt", AF.Exp), waits=[g8, g9], inc="se")
        gb = p.emit("vector", stt("g2", "g2", 1.0, "g1", OP.max, OP.mult), waits=[ga], inc="ve")
        gc = p.emit("vector", red(("m1c", np.s_[:, 0:1]), "g2", OP.add), waits=[gb], inc="ve")
        gd = p.emit("vector", recip(("m1c", np.s_[:, 0:1]), ("m1c", np.s_[:, 0:1])), waits=[gc], inc="ve")
        ge_ = p.emit("vector", ts1(("wtok", np.s_[:, tt_ * NE:(tt_ + 1) * NE]), "g2",
                                   ("m1c", np.s_[:, 0:1]), OP.mult), waits=[gd], inc="ve")
        p.war.update(m1c=ge_, m2c=g7, g1=gb, g2=ge_, lgt=ga, ibit=g9)
        th = p.emit("tensor", tr(("P5", np.s_[0:NE, 0:128]),
                                 ("wtok", np.s_[:, tt_ * NE:(tt_ + 1) * NE]), "ident_f"),
                    waits=[ge_, p.ident_tok, p.war.get("P5")], inc="pe")
        ti_ = p.emit("vector", cp(("wT", np.s_[0:16, tt_ * 128:(tt_ + 1) * 128]),
                                  ("P5", np.s_[0:NE, 0:128])), waits=[th], inc="ve")
        p.war["P5"] = ti_
    w_tok = p.now("ve")
    tok_w_out = p.emit("sync", lambda e, ap: e.dma_start(
        out=ap["w_out"].rearrange("(q p) e -> p q e", p=128),
        in_=ap["wtok"][:, :].rearrange("p (q e) -> p q e", q=8)),
        waits=[w_tok], inc="dmo", add=16)
    out_toks = [tok_w_out]
    for pj, (wnm, outnm, tok_major) in enumerate(
            (("qproj", "qTf_out", False), ("kproj", "kTf_out", False), ("vproj", "vf_out", True))):
        for e in range(NE):
            par = e % 2
            mxw = "mxw%d" % par
            ld = p.emit("gpsimd", lambda e_, ap, _w=wnm, _e=e, _m=mxw: e_.dma_start(
                out=ap[_m][:, :].rearrange("p (i h) -> p i h", i=16),
                in_=ap[_w][_e].rearrange("(i p) h -> p i h", p=128)),
                waits=[p.war.get((mxw,))], inc="dmg", add=16)
            for tg in range(2):
                tb = p.emit("tensor", mm("P4", ("oh", np.s_[0:16, e * 128:(e + 1) * 128]),
                                         ("wT", np.s_[0:16, tg * CH:(tg + 1) * CH]),
                                         start=True, stop=True),
                            waits=[w_tok, ld_oh, p.war.get("P4")], inc="pe")
                tc = p.emit("scalar", act("wbe", "P4", AF.Identity),
                            waits=[tb, p.war.get("wbe")], inc="se")
                p.war["P4"] = tc
                ps = "P%d" % (tg % 2)
                for i in range(16):
                    t_mm = p.emit("tensor", mm(ps, (mxw, np.s_[:, i * 128:(i + 1) * 128]),
                                               (xn, xsl(i, tg)), start=(i == 0), stop=(i == 15)),
                                  waits=[ld, xn_tok, p.war.get(ps) if i == 0 else None], inc="pe")
                tm = p.emit("vector", tt("mxt", ps, "wbe", OP.mult),
                            waits=[t_mm, tc, p.war.get("mxt")], inc="ve")
                p.war[ps] = tm
                p.war["wbe"] = tm
                if e == 0:
                    ta = p.emit("vector", cp(("macc", np.s_[:, tg * CH:(tg + 1) * CH]), "mxt"),
                                waits=[tm, p.war.get("macc")], inc="ve")
                else:
                    ta = p.emit("vector", tt(("macc", np.s_[:, tg * CH:(tg + 1) * CH]),
                                             ("macc", np.s_[:, tg * CH:(tg + 1) * CH]), "mxt", OP.add),
                                waits=[tm], inc="ve")
                p.war["mxt"] = ta
            p.war[(mxw,)] = p.now("pe")
        acc_tok = p.now("ve")
        if not tok_major:
            tq = p.emit("vector", cp("qfb", "macc"), waits=[acc_tok, p.war.get("qfb")], inc="ve")
            p.war["macc"] = tq
            td = p.emit("sync", lambda e, ap, _o=outnm: e.dma_start(out=ap[_o][:, :], in_=ap["qfb"][:, :]),
                        waits=[tq], inc="dmo", add=16)
            p.war["qfb"] = td
            out_toks.append(td)
        else:
            for tt_ in range(8):
                t_t = p.emit("tensor", tr(("P5", np.s_[:, 0:128]),
                                          ("macc", np.s_[:, tt_ * 128:(tt_ + 1) * 128]), "ident_f"),
                             waits=[acc_tok, p.ident_tok, p.war.get("P5")], inc="pe")
                t_e = p.emit("scalar", act("vstg", ("P5", np.s_[:, 0:128]), AF.Identity),
                             waits=[t_t, p.war.get("vstg")], inc="se")
                p.war["P5"] = t_e
                t_d = p.emit("sync", lambda e, ap, _o=outnm, _t=tt_: e.dma_start(
                    out=ap[_o][_t * 128:(_t + 1) * 128, :], in_=ap["vstg"][:, :]),
                    waits=[t_e], inc="dmo", add=16)
                p.war["vstg"] = t_d
                out_toks.append(t_d)
            p.war["macc"] = p.now("pe")
    return out_toks



def make_relay_writer(p, outmap, bias_sb):
    state = {"i": 0}

    def writer(oc, tg, ps, tok_mm):
        par = state["i"] % 2
        state["i"] += 1
        st = "pst%d" % par
        t_a = p.emit("scalar", act(st, ps, AF.Identity,
                                   bias=(bias_sb, np.s_[:, oc:oc + 1]) if bias_sb else None),
                     waits=[tok_mm, p.war.get(st), getattr(p, "qkbias_tok", None)], inc="se")
        nm, r0 = outmap(oc)
        tok_d = p.dma("sync", (nm, np.s_[r0:r0 + 128, tg * CH:(tg + 1) * CH]), st,
                      waits=[t_a], inc="dmo")
        p.war[st] = tok_d
        p.out_toks.append(tok_d)
        return t_a

    return writer


def make_rope_relay_writer(p, outmap, bias_sb, bias_sw_sb, cos_sb, sin_sb):
    state = {"i": 0}

    def writer(oc, tg, ps, tok_mm):
        par = state["i"] % 2
        state["i"] += 1
        bias = (bias_sb, np.s_[:, oc:oc + 1]) if bias_sb else None
        t0 = p.emit("scalar", act("lntmp", ps, AF.Identity, bias=bias),
                    waits=[tok_mm, p.war.get("lntmp"), getattr(p, "qkbias_tok", None)], inc="se")
        bsw0 = (bias_sw_sb, np.s_[0:64, oc:oc + 1]) if bias_sw_sb else None
        bsw1 = (bias_sw_sb, np.s_[64:128, oc:oc + 1]) if bias_sw_sb else None
        ts0 = p.emit("scalar", act(("sq0", np.s_[0:64, :]), (ps, np.s_[64:128, :]),
                                   AF.Identity, bias=bsw0),
                     waits=[tok_mm, p.war.get("sq0")], inc="se")
        ts1 = p.emit("scalar", act(("sq0", np.s_[64:128, :]), (ps, np.s_[0:64, :]),
                                   AF.Identity, bias=bsw1),
                     waits=[tok_mm], inc="se")
        csl = np.s_[:, tg * CH:(tg + 1) * CH]
        ta = p.emit("vector", tt("sq1", "lntmp", (cos_sb, csl), OP.mult),
                    waits=[t0, p.war.get("sq1"), p.rope_tok], inc="ve")
        tb = p.emit("vector", tt("Mb", "sq0", (sin_sb, csl), OP.mult),
                    waits=[ts0, ts1, p.war.get("Mb")], inc="ve")
        p.war["lntmp"] = ta
        p.war["sq0"] = tb
        st = "pst%d" % par
        tok = p.emit("vector", tt(st, "sq1", "Mb", OP.add),
                     waits=[ta, tb, p.war.get(st)], inc="ve")
        p.war["sq1"] = tok
        p.war["Mb"] = tok
        nm, r0 = outmap(oc)
        tok_d = p.dma("sync", (nm, np.s_[r0:r0 + 128, tg * CH:(tg + 1) * CH]), st,
                      waits=[tok], inc="dmo")
        p.war[st] = tok_d
        p.out_toks.append(tok_d)
        return tok

    return writer


def make_v_writer(p, out_dram, bvb):
    def writer(cb, tt_, ps, tok_mm, wcols):
        par = (cb * 8 + tt_) % 2
        st = "vst%d" % par
        ssl = np.s_[:, 0:wcols]
        if bvb:
            t_a = p.emit("vector", tt((st, ssl), (ps, ssl),
                                      (bvb, np.s_[:, cb * wcols:(cb + 1) * wcols]), OP.add),
                         waits=[tok_mm, p.war.get(st), p.bvb_tok], inc="ve")
        else:
            t_a = p.emit("scalar", act((st, ssl), (ps, ssl), AF.Identity),
                         waits=[tok_mm, p.war.get(st)], inc="se")
        tok_d = p.dma("sync",
                      (out_dram, np.s_[tt_ * 128:(tt_ + 1) * 128, cb * wcols:(cb + 1) * wcols]),
                      (st, ssl), waits=[t_a], inc="dmo")
        p.war[st] = tok_d
        p.out_toks.append(tok_d)
        return t_a

    return writer


def emit_bvb(p, brow_dram, ncols):
    ld = p.dma("gpsimd", ("bvrow_sb", np.s_[0:1, 0:ncols]), (brow_dram, None))
    for cb in range((ncols + CH - 1) // CH):
        cw = min(CH, ncols - cb * CH)
        t_b = p.emit("tensor", mm(("P3", np.s_[:, 0:cw]), ("ones1r", np.s_[0:1, :]),
                                  ("bvrow_sb", np.s_[0:1, cb * CH:cb * CH + cw]),
                                  start=True, stop=True),
                     waits=[ld, p.const_tok, p.war.get("P3")], inc="pe")
        t_c = p.emit("vector", cp(("bvb", np.s_[:, cb * CH:cb * CH + cw]), ("P3", np.s_[:, 0:cw])),
                     waits=[t_b], inc="ve")
        p.war["P3"] = t_c
    p.bvb_tok = p.now("ve")


def declare_relay_stages(p):
    p.sbuf("pst0", [128, CH], BF16)
    p.sbuf("pst1", [128, CH], BF16)
    p.sbuf("vst0", [128, CH], BF16)
    p.sbuf("vst1", [128, CH], BF16)


def declare_ident(p):
    p.dram("ident_in", [128, 128], F32, "ExternalInput")
    p.sbuf("ident_f", [128, 128], F32)
    p.dma("gpsimd", "ident_f", "ident_in")
    p.ident_tok = p.now("dmg")


def declare_rope(p):
    p.dram("cosT_in", [128, NT], F32, "ExternalInput")
    p.dram("sinT_in", [128, NT], F32, "ExternalInput")
    p.sbuf("cosT", [128, NT], F32)
    p.sbuf("sinT", [128, NT], F32)
    p.dma("gpsimd", "cosT", "cosT_in")
    p.dma("gpsimd", "sinT", "sinT_in")
    p.rope_tok = p.now("dmg")


def emit_qkv_relay(p, xn, xn_tok, prefix, rope, nq=16, nk=16, vcols=D, kv_only=False):
    qk_cols = (nq + nk) * 128
    p.dram(f"wqk_{prefix}", [D, qk_cols], F32, "ExternalInput")
    p.dram(f"bqk_{prefix}", [128, nq + nk], F32, "ExternalInput")
    p.dram(f"wv_{prefix}", [D, vcols], F32, "ExternalInput")
    p.dram(f"bv_{prefix}", [1, vcols], F32, "ExternalInput")
    if not kv_only:
        p.dram(f"QT{prefix}_out", [nq * 128, NT], BF16, "ExternalOutput")
    p.dram(f"KT{prefix}_out", [nk * 128, NT], BF16, "ExternalOutput")
    p.dram(f"V{prefix}_out", [NT, vcols], BF16, "ExternalOutput")

    def outmap(oc):
        if oc < nq:
            return (f"QT{prefix}_out", oc * 128)
        return (f"KT{prefix}_out", (oc - nq) * 128)

    has_bias = True
    if rope:
        p.dram(f"bqksw_{prefix}", [128, nq + nk], F32, "ExternalInput")
        p.sbuf(f"bqksw_sb_{prefix}", [128, nq + nk], F32)
        p.dma("gpsimd", f"bqksw_sb_{prefix}", f"bqksw_{prefix}")
        writer = make_rope_relay_writer(p, outmap, f"bqk_sb_{prefix}", f"bqksw_sb_{prefix}",
                                        "cosT", "sinT")
    else:
        writer = make_relay_writer(p, outmap, f"bqk_sb_{prefix}")
    p.sbuf(f"bqk_sb_{prefix}", [128, nq + nk], F32)
    p.dma("gpsimd", f"bqk_sb_{prefix}", f"bqk_{prefix}")
    p.qkbias_tok = p.now("dmg")
    wslot_qk = 256 if qk_cols % 256 == 0 else 128
    emit_proj_fm(p, xn, xn_tok, f"wqk_{prefix}", 0, qk_cols, writer, "wark", wslot_qk, prefix)
    emit_bvb(p, f"bv_{prefix}", vcols)
    vw = make_v_writer(p, f"V{prefix}_out", "bvb")
    wslot_v = 256 if vcols % 256 == 0 else 128
    emit_proj_tok(p, xn, xn_tok, f"wv_{prefix}", 0, vcols, "bvb", vw, "wark", wslot_v, prefix)



def build_L1():
    p = Prog("L1")
    declare_common(p, psum_big=6)
    declare_ident(p)
    declare_relay_stages(p)
    p.out_toks = []
    p.sbuf("bufA", [128, 16 * NT], BF16)
    p.sbuf("wark", [128, 2 * 16 * 256], BF16)
    p.sbuf("bvrow_sb", [1, D], F32)
    p.sbuf("bvb", [128, D], BF16)
    for nm, r, c_ in W_SLICES:
        p.dram(nm, [r, c_], F32, "ExternalInput")
    p.dram("gpart", [8, 4], F32, "ExternalInput")
    p.dram("q1T_out", [256, HFF], BF16, "ExternalOutput")
    p.dram("q2T_out", [256, HFF], BF16, "ExternalOutput")
    p.dram("q3T_out", [HFF, 256], BF16, "ExternalOutput")
    p.sbuf("gpart_sb", [8, 4], F32)
    p.sbuf("srow", [1, 4], F32)
    p.sbuf("scol", [128, 3], F32)
    p.sbuf("nscol", [128, 3], F32)
    p.sbuf("wq0", [128, 512], F32)
    p.sbuf("wq1", [128, 512], F32)
    p.sbuf("qa", [128, 512], F32)
    p.sbuf("qb", [128, 512], F32)
    p.sbuf("qq", [128, 512], F32)
    p.sbuf("qst", [128, 2 * HFF], BF16)

    xn_tok = emit_ln(p, "bufA", first=True)
    p.ln_done = xn_tok
    emit_qkv_relay(p, "bufA", xn_tok, "l0a", rope=False)
    gp_tok = p.dma("gpsimd", ("gpart_sb", np.s_[0:8, 0:4]), ("gpart", None))
    emit_quantize(p, gp_tok)
    final_waits(p, p.out_toks + p.q_out_toks)
    return p


def build_B(name, cfg):
    p = Prog(name)
    declare_common(p, psum_big=6, xt_out=True)
    p.out_toks = []
    tag = cfg["tag"]
    causal = cfg["causal"]
    mqa = cfg.get("mqa", False)
    single = cfg.get("single", False)
    declare_attn(p, tag, causal, mqa=mqa or single, first=True)
    p.sbuf("bufA", [128, 16 * NT], BF16)
    if not single:
        p.sbuf("bufB", [128, (44 * CH) if cfg.get("ffn") else (16 * NT)], BF16)
    p.sbuf("wark", [128, 2 * 16 * 256], BF16)
    kv_tok = load_mqa_kv(p, tag) if (mqa or single) else None

    if single:
        p.dram("qTf_in", [128, NT], BF16, "ExternalInput")
        p.dram("wT_in", [16, NT], F32, "ExternalInput")
        p.dram("onehot", [NE, NE * 128], F32, "ExternalInput")
        p.sbuf("qf", [128, NT], BF16)
        p.sbuf("wT", [16, NT], F32)
        p.sbuf("oh", [16, NE * 128], F32)
        p.sbuf("aoT", [128, NT], BF16)
        p.sbuf("wbe", [128, CH], F32)
        p.dma("sync", "qf", "qTf_in")
        p.dma("sync", "wT", "wT_in")
        ld_oh = p.dma("gpsimd", ("oh", np.s_[0:16, :]), ("onehot", None))
        qt_tok = p.now("dms")
        at_tok = emit_attn(p, tag, "qf", qt_tok, "aoT", causal, nheads=1, mqa=True,
                           kv_res_tok=kv_tok)
        for e in range(NE):
            for tg in range(2):
                tb = p.emit("tensor", mm(("P0", np.s_[:, 0:CH]),
                                         ("oh", np.s_[0:16, e * 128:(e + 1) * 128]),
                                         ("wT", np.s_[0:16, tg * CH:(tg + 1) * CH]),
                                         start=True, stop=True),
                            waits=[qt_tok, ld_oh, p.war.get("P0")], inc="pe")
                tc = p.emit("scalar", act("wbe", "P0", AF.Identity),
                            waits=[tb, p.war.get("wbe")], inc="se")
                p.war["P0"] = tc
                tok = p.emit("vector", tt(("bufA", xsl(e, tg)),
                                          ("aoT", np.s_[:, tg * CH:(tg + 1) * CH]), "wbe", OP.mult),
                             waits=[tc, at_tok, p.war.get("bufA") if (e == 0 and tg == 0) else None],
                             inc="ve")
                p.war["wbe"] = tok
        aow_tok = p.now("ve")
        p.dram(f"wo_{tag}", [D, D], F32, "ExternalInput")
        emit_out_proj(p, "bufA", aow_tok, f"wo_{tag}", None, "wark", tag)
    else:
        p.dram("QT_in", [D, NT], BF16, "ExternalInput")
        qt_tok = p.emit("sync", lambda e, ap: e.dma_start(
            out=ap["bufA"][:, :].rearrange("p (i t) -> p i t", i=16),
            in_=ap["QT_in"].rearrange("(i p) t -> p i t", p=128)),
            waits=[w for w in [p.war.get("bufA")] if w], inc="dms", add=16)
        at_tok = emit_attn(p, tag, "bufA", qt_tok, "bufB", causal, nheads=NH, mqa=mqa,
                           kv_res_tok=kv_tok)
        p.dram(f"wo_{tag}", [D, D], F32, "ExternalInput")
        bias_sb = None
        if cfg.get("bo", False):
            p.dram(f"bo_{tag}", [128, 16], F32, "ExternalInput")
            p.sbuf(f"bo_sb_{tag}", [128, 16], F32)
            p.dma("gpsimd", f"bo_sb_{tag}", f"bo_{tag}")
            bias_sb = f"bo_sb_{tag}"
        emit_out_proj(p, "bufB", at_tok, f"wo_{tag}", bias_sb, "wark", tag)

    if cfg.get("ffn", False):
        declare_ffn(p, first=False)
        p.sbuf("sgt0", [128, CH], BF16)
        p.sbuf("sgt1", [128, CH], BF16)
        xn_tok = emit_ln(p, "bufA", first=True)
        emit_ffn_arena(p, "bufA", xn_tok, hb="bufB")

    xt_tok_out = p.emit("sync", lambda e, ap: e.dma_start(
        out=ap["xT_out"].rearrange("(i p) t -> p i t", p=128),
        in_=ap["xT"][:, :].rearrange("p (i t) -> p i t", i=16)),
        waits=[p.now("ve")], inc="dmo", add=16)
    final_waits(p, p.out_toks + [xt_tok_out])
    return p


def emit_ffn_arena(p, xn, xn_tok, hb):
    A = "wark"
    for tg in range(2):
        bar = [p.war.get((A, 0)), p.war.get((A, 1)), p.war.get((A, "q3", 0)), p.war.get((A, "q3", 1))]
        for hc in range(44):
            par = hc % 2
            s1 = np.s_[:, par * 2048:(par + 1) * 2048]
            s2 = np.s_[:, (2 + par) * 2048:(3 + par) * 2048]
            ld1 = p.emit("sync", lambda e, ap, _hc=hc, _sl=s1: e.dma_start(
                out=ap[A][_sl].rearrange("p (i j) -> p i j", i=16),
                in_=ap["Q1T"][:, _hc * 128:(_hc + 1) * 128]
                .rearrange("(i p) j -> p i j", p=128)),
                waits=[p.war.get((A, 0) if par == 0 else (A, 1))] + (bar if hc < 2 else []),
                inc="dms", add=16)
            ld2 = p.emit("scalar", lambda e, ap, _hc=hc, _sl=s2: e.dma_start(
                out=ap[A][_sl].rearrange("p (i j) -> p i j", i=16),
                in_=ap["Q2T"][:, _hc * 128:(_hc + 1) * 128]
                .rearrange("(i p) j -> p i j", p=128)),
                waits=[p.war.get((A, 2) if par == 0 else (A, 3))] + (bar if hc < 2 else []),
                inc="dmc", add=16)
            px1 = ("P0", "P3")[par]
            px2 = ("P1", "P4")[par]
            for i in range(16):
                t1 = p.emit("tensor", mm(px1, (A, np.s_[:, par * 2048 + i * 128:
                                                        par * 2048 + (i + 1) * 128]),
                                         (xn, xsl(i, tg)), start=(i == 0), stop=(i == 15)),
                            waits=[ld1, xn_tok, p.war.get(px1) if i == 0 else None], inc="pe")
                t2 = p.emit("tensor", mm(px2, (A, np.s_[:, (2 + par) * 2048 + i * 128:
                                                        (2 + par) * 2048 + (i + 1) * 128]),
                                         (xn, xsl(i, tg)), start=(i == 0), stop=(i == 15)),
                            waits=[ld2, p.war.get(px2) if i == 0 else None], inc="pe")
            p.war[(A, 0) if par == 0 else (A, 1)] = t1
            p.war[(A, 2) if par == 0 else (A, 3)] = t2
            t_s = p.emit("scalar", act("sgt%d" % par, px1, AF.Sigmoid),
                         waits=[t1, p.war.get("sgt%d" % par)], inc="se")
            p.war[px1] = t_s
            tok_h = p.emit("vector", tt((hb, np.s_[:, hc * CH:(hc + 1) * CH]), px2, "sgt%d" % par,
                                        OP.mult),
                           waits=[t_s, t2, p.war.get(hb) if (hc == 0 and tg == 0) else None], inc="ve")
            p.war[px2] = tok_h
            p.war["sgt%d" % par] = tok_h
        hb_tok = p.now("ve")
        bar2 = [p.war.get((A, 0)), p.war.get((A, 1)), p.war.get((A, 2)), p.war.get((A, 3))]
        for dc in range(16):
            lds = [None, None]
            for h2 in range(2):
                par = (dc * 2 + h2) % 2
                s3 = np.s_[:, par * 4096: par * 4096 + 22 * 128]
                lds[h2] = p.emit("gpsimd", lambda e, ap, _dc=dc, _h2=h2, _sl=s3: e.dma_start(
                    out=ap[A][_sl].rearrange("p (i j) -> p i j", i=22),
                    in_=ap["Q3T"][_h2 * 22 * 128:(_h2 + 1) * 22 * 128, _dc * 128:(_dc + 1) * 128]
                    .rearrange("(i p) j -> p i j", p=128)),
                    waits=[p.war.get((A, "q3", par))] + (bar2 if dc == 0 else []),
                    inc="dmg", add=16)
            pw = ("P5", "P3")[dc % 2]
            for hc in range(44):
                h2 = hc // 22
                par = (dc * 2 + h2) % 2
                tok_mm = p.emit("tensor", mm(pw, (A, np.s_[:, par * 4096 + (hc % 22) * 128:
                                                           par * 4096 + (hc % 22 + 1) * 128]),
                                             (hb, np.s_[:, hc * CH:(hc + 1) * CH]),
                                             start=(hc == 0), stop=(hc == 43)),
                                waits=[lds[h2], hb_tok, p.war.get(pw) if hc == 0 else None], inc="pe")
                if hc % 22 == 21:
                    p.war[(A, "q3", par)] = tok_mm
            tok = p.emit("vector", tt(("xT", xsl(dc, tg)), pw, ("xT", xsl(dc, tg)), OP.add),
                         waits=[tok_mm], inc="ve")
            p.war[pw] = tok
        p.war[hb] = p.now("pe")
    return p.now("ve")


def build_A(name, cfg):
    p = Prog(name)
    declare_common(p, psum_big=6, xt_out=False)
    declare_ident(p)
    declare_relay_stages(p)
    p.out_toks = []
    p.sbuf("bufA", [128, 16 * NT], BF16)
    p.has_sq1 = True
    if cfg["kind"] != "qkv":
        p.vrow_full = True
    xn_tok = emit_ln(p, "bufA", first=True)
    p.ln_done = xn_tok
    if cfg["kind"] == "qkv":
        p.sbuf("wark", [128, 2 * 16 * 256], BF16)
        if cfg.get("rope"):
            declare_rope(p)
        vcols = cfg.get("vcols", D)
        p.sbuf("bvrow_sb", [1, vcols], F32)
        p.sbuf("bvb", [128, vcols], BF16)
        emit_qkv_relay(p, "bufA", xn_tok, cfg["prefix"], rope=cfg.get("rope", False),
                       nq=cfg.get("nq", 16), nk=cfg.get("nk", 16), vcols=vcols)
    else:
        declare_l1ffn_A(p, first=True)
        toks = emit_l1ffn_A(p, "bufA", xn_tok)
        p.out_toks.extend(toks)
    final_waits(p, p.out_toks)
    return p


B_CFGS = {
    "B1": dict(tag="l0a", causal=True, bo=True, ffn=True),
    "B2": dict(tag="l0b", causal=False, bo=True, ffn=True),
    "B3": dict(tag="l1a", causal=True, bo=False, ffn=False),
    "B4": dict(tag="lf1", causal=True, single=True, ffn=False),
    "B5": dict(tag="l1b", causal=True, mqa=True, bo=True, ffn=False),
    "B6": dict(tag="lf2", causal=True, single=True, ffn=False),
}
A_CFGS = {
    "A2": dict(kind="qkv", prefix="l0b", rope=True),
    "A3": dict(kind="qkv", prefix="l1a", rope=True),
    "A4": dict(kind="l1ffn"),
    "A5": dict(kind="qkv", prefix="l1b", rope=True, nq=16, nk=1, vcols=128),
    "A6": dict(kind="l1ffn"),
}

_PROG_CACHE = {}
PROBE = None


def get_prog(name):
    if name not in _PROG_CACHE:
        if name == "L0":
            p = build_L0()
        elif name == "L1":
            p = build_L1()
        elif name.startswith("B"):
            p = build_B(name, B_CFGS[name])
        else:
            p = build_A(name, A_CFGS[name])
        _PROG_CACHE[name] = p.build()
    return _PROG_CACHE[name]



def _bsch(c):
    b = np.full((128, 24), -CSH, dtype=np.float32)
    g0, g1 = own_chunks(c)
    for s in range(7):
        if s >= g0:
            b[:, s] = NEGBIG
    for s in range(15):
        if s >= g1:
            b[:, 8 + s] = NEGBIG
    return b


def _tri():
    k = np.arange(128)
    return (k[None, :] >= k[:, None]).astype(np.float32)


def _gather_kv(res, kname, vname, vcols=D):
    krows = res[0][kname].shape[0]
    KT = np.zeros((krows, T), dtype=BF)
    V = np.zeros((T, vcols), dtype=BF)
    for c in range(NCORE):
        idx = own_token_idx(c)
        KT[:, idx] = res[c][kname]
        V[idx, :] = res[c][vname]
    return KT, V


def kernel(**ins):
    ins = {k: np.asarray(v) for k, v in ins.items()}
    x0 = ins["L0_attn_a_x"].reshape(T, D).astype(np.float32)
    xT = np.ascontiguousarray(x0.T)
    perm = deinterleave_perm()
    ident = np.eye(128, dtype=np.float32)
    tri = _tri()
    onehot = np.zeros((NE, NE * 128), dtype=np.float32)
    for e in range(NE):
        onehot[e, e * 128:(e + 1) * 128] = 1.0

    cores = list(range(NCORE))
    own_idx = [own_token_idx(c) for c in cores]
    bsch = [_bsch(c) for c in cores]
    rope_tabs = {k: [rope_tables(k, own_idx[c]) for c in cores]
                 for k in ("concat", "il_a", "il_b")}

    def xslices():
        return [np.ascontiguousarray(xT[:, own_idx[c]]) for c in cores]

    def scatter_x(res, _n=[0]):
        for c in cores:
            xT[:, own_idx[c]] = res[c]["xT_out"]
        _n[0] += 1
        if PROBE is not None:
            PROBE["x%d" % _n[0]] = xT.copy()

    CHECK_OUT = {"L0": "partials", "L1": "KTl0a_out", "B1": "xT_out", "B2": "xT_out",
                 "B3": "xT_out", "B4": "xT_out", "B5": "xT_out", "B6": "xT_out",
                 "A2": "KTl0b_out", "A3": "KTl1a_out", "A4": "kTf_out",
                 "A5": "KTl1b_out", "A6": "kTf_out"}

    def run(name, in_maps):
        nc = get_prog(name)
        chk = CHECK_OUT[name]
        for attempt in range(3):
            res = run_bass_kernel_spmd(nc, in_maps, core_ids=cores).results
            sample = np.asarray(res[0][chk]).view(np.uint16 if res[0][chk].dtype.itemsize == 2
                                                 else np.uint32)
            if sample.any():
                return res
        raise RuntimeError(f"launch {name}: all-zero outputs after retries (transient)")

    w1 = ins["L0_ffn_w1"].astype(np.float32)
    w2 = ins["L0_ffn_w2"].astype(np.float32)
    w3 = ins["L0_ffn_w3"].astype(np.float32)
    w_slices = [dict(
        w1s=np.ascontiguousarray(w1[:, c * 256:(c + 1) * 256]),
        w2s=np.ascontiguousarray(w2[:, c * 256:(c + 1) * 256]),
        w3s=np.ascontiguousarray(w3[c * 256:(c + 1) * 256, :]),
    ) for c in cores]
    res0 = run("L0", [dict(w_slices[c]) for c in cores])
    gpart = np.ascontiguousarray(
        np.concatenate([res0[c]["partials"] for c in cores], axis=0))

    wqkv = ins["L0_attn_a_w_qkv"].astype(np.float32)
    bqkv = ins["L0_attn_a_b_qkv"].astype(np.float32)
    res1 = run("L1", [dict(
        xT_in=xs, ident_in=ident,
        wqk_l0a=np.ascontiguousarray(wqkv[:, 0:4096]),
        bqk_l0a=pack_bias(bqkv[0:4096]),
        wv_l0a=np.ascontiguousarray(wqkv[:, 4096:6144]),
        bv_l0a=bqkv[4096:6144].reshape(1, -1),
        gpart=gpart, **w_slices[c]) for c, xs in zip(cores, xslices())])
    Q1T = np.concatenate([res1[c]["q1T_out"] for c in cores], axis=0)
    Q2T = np.concatenate([res1[c]["q2T_out"] for c in cores], axis=0)
    Q3T = np.concatenate([res1[c]["q3T_out"] for c in cores], axis=1)
    KT1, V1 = _gather_kv(res1, "KTl0a_out", "Vl0a_out")

    res2 = run("B1", [dict(
        xT_in=xs, tri_in=tri,
        QT_in=res1[c]["QTl0a_out"], KT_l0a=KT1, V_l0a=V1,
        KTd_l0a=res1[c]["KTl0a_out"], Vd_l0a=res1[c]["Vl0a_out"],
        bsch_l0a=bsch[c],
        wo_l0a=ins["L0_attn_a_w_proj"].astype(np.float32),
        bo_l0a=pack_bias(ins["L0_attn_a_b_proj"].astype(np.float32)),
        Q1T=Q1T, Q2T=Q2T, Q3T=Q3T) for c, xs in zip(cores, xslices())])
    scatter_x(res2)

    resA = run("A2", [dict(
        xT_in=xs, ident_in=ident,
        wqk_l0b=np.ascontiguousarray(np.concatenate(
            [ins["L0_attn_b_Wq"].T, ins["L0_attn_b_Wk"].T], axis=1).astype(np.float32)),
        bqk_l0b=pack_bias(np.concatenate(
            [ins["L0_attn_b_bq"], ins["L0_attn_b_bk"]]).astype(np.float32)),
        bqksw_l0b=swap_halves(pack_bias(np.concatenate(
            [ins["L0_attn_b_bq"], ins["L0_attn_b_bk"]]).astype(np.float32))),
        wv_l0b=np.ascontiguousarray(ins["L0_attn_b_Wv"].T.astype(np.float32)),
        bv_l0b=ins["L0_attn_b_bv"].reshape(1, -1).astype(np.float32),
        cosT_in=rope_tabs["concat"][c][0], sinT_in=rope_tabs["concat"][c][1])
        for c, xs in zip(cores, xslices())])
    KT2, V2 = _gather_kv(resA, "KTl0b_out", "Vl0b_out")
    QT2 = [resA[c]["QTl0b_out"] for c in cores]

    res2 = run("B2", [dict(
        xT_in=xs, tri_in=tri,
        QT_in=QT2[c], KT_l0b=KT2, V_l0b=V2,
        wo_l0b=np.ascontiguousarray(ins["L0_attn_b_Wo"].T.astype(np.float32)),
        bo_l0b=pack_bias(ins["L0_attn_b_bo"].astype(np.float32)),
        Q1T=Q1T, Q2T=Q2T, Q3T=Q3T) for c, xs in zip(cores, xslices())])
    scatter_x(res2)

    wq_l1a = permute_head_cols(ins["L1_attn_a_Wq"].T.astype(np.float32), perm)
    wk_l1a = permute_head_cols(ins["L1_attn_a_Wk"].T.astype(np.float32), perm)
    resA = run("A3", [dict(
        xT_in=xs, ident_in=ident,
        wqk_l1a=np.ascontiguousarray(np.concatenate([wq_l1a, wk_l1a], axis=1)),
        bqk_l1a=np.zeros((128, 32), dtype=np.float32),
        bqksw_l1a=np.zeros((128, 32), dtype=np.float32),
        wv_l1a=np.ascontiguousarray(ins["L1_attn_a_Wv"].T.astype(np.float32)),
        bv_l1a=np.zeros((1, D), dtype=np.float32),
        cosT_in=rope_tabs["il_a"][c][0], sinT_in=rope_tabs["il_a"][c][1])
        for c, xs in zip(cores, xslices())])
    KT3, V3 = _gather_kv(resA, "KTl1a_out", "Vl1a_out")
    QT3 = [resA[c]["QTl1a_out"] for c in cores]
    resAp = resA

    res2 = run("B3", [dict(
        xT_in=xs, tri_in=tri,
        QT_in=QT3[c], KT_l1a=KT3, V_l1a=V3,
        KTd_l1a=resAp[c]["KTl1a_out"], Vd_l1a=resAp[c]["Vl1a_out"],
        bsch_l1a=bsch[c],
        wo_l1a=np.ascontiguousarray(ins["L1_attn_a_Wo"].T.astype(np.float32)))
        for c, xs in zip(cores, xslices())])
    scatter_x(res2)

    sim = ins["L1_ffn_sim_matrix"].astype(np.float32)
    gates = ins["L1_ffn_gates"].reshape(1, NE).astype(np.float32)
    qproj = ins["L1_ffn_q_proj"].astype(np.float32)
    kproj = ins["L1_ffn_k_proj"].astype(np.float32)
    vproj = ins["L1_ffn_v_proj"].astype(np.float32)
    OPr = np.ascontiguousarray(ins["L1_ffn_o_proj"].astype(np.float32).reshape(D, D))

    def run_l1ffn(aname, bname, atag):
        resA = run(aname, [dict(
            xT_in=xs, ident_in=ident, sim=sim, gates=gates,
            qproj=qproj, kproj=kproj, vproj=vproj, onehot=onehot)
            for c, xs in zip(cores, xslices())])
        KTf, Vf = _gather_kv(resA, "kTf_out", "vf_out", vcols=128)
        resB = run(bname, [dict(
            xT_in=xs, tri_in=tri,
            qTf_in=resA[c]["qTf_out"],
            wT_in=np.ascontiguousarray(resA[c]["w_out"].astype(np.float32).T),
            onehot=onehot,
            **{f"KT_{atag}": KTf, f"V_{atag}": Vf,
               f"KTd_{atag}": resA[c]["kTf_out"], f"Vd_{atag}": resA[c]["vf_out"],
               f"bsch_{atag}": bsch[c], f"wo_{atag}": OPr})
            for c, xs in zip(cores, xslices())])
        scatter_x(resB)

    run_l1ffn("A4", "B4", "lf1")

    wq_l1b = permute_head_cols(ins["L1_attn_b_Wq"].astype(np.float32), perm)
    bq_l1b = permute_head_cols(ins["L1_attn_b_bq"].astype(np.float32)[None, :], perm)[0]
    wk_l1b = permute_head_cols(ins["L1_attn_b_Wk"].astype(np.float32), perm)
    bk_l1b = permute_head_cols(ins["L1_attn_b_bk"].astype(np.float32)[None, :], perm)[0]
    resA = run("A5", [dict(
        xT_in=xs, ident_in=ident,
        wqk_l1b=np.ascontiguousarray(np.concatenate([wq_l1b, wk_l1b], axis=1)),
        bqk_l1b=pack_bias(np.concatenate([bq_l1b, bk_l1b])),
        bqksw_l1b=swap_halves(pack_bias(np.concatenate([bq_l1b, bk_l1b]))),
        wv_l1b=np.ascontiguousarray(ins["L1_attn_b_Wv"].astype(np.float32)),
        bv_l1b=ins["L1_attn_b_bv"].reshape(1, 128).astype(np.float32),
        cosT_in=rope_tabs["il_b"][c][0], sinT_in=rope_tabs["il_b"][c][1])
        for c, xs in zip(cores, xslices())])
    KT5, V5 = _gather_kv(resA, "KTl1b_out", "Vl1b_out", vcols=128)
    QT5 = [resA[c]["QTl1b_out"] for c in cores]
    resAp = resA

    res2 = run("B5", [dict(
        xT_in=xs, tri_in=tri,
        QT_in=QT5[c], KT_l1b=KT5, V_l1b=V5,
        KTd_l1b=resAp[c]["KTl1b_out"], Vd_l1b=resAp[c]["Vl1b_out"],
        bsch_l1b=bsch[c],
        wo_l1b=np.ascontiguousarray(ins["L1_attn_b_Wo"].astype(np.float32)),
        bo_l1b=pack_bias(ins["L1_attn_b_bo"].astype(np.float32)))
        for c, xs in zip(cores, xslices())])
    scatter_x(res2)

    run_l1ffn("A4", "B4", "lf1")

    return np.ascontiguousarray(xT.T).reshape(1, T, D).astype(np.float32)
